# version 11 (final)
# speedup vs baseline: 1.3646x; 1.1297x over previous
import numpy as np
import ml_dtypes
import concourse.bass as bass
import concourse.mybir as mybir
from concourse.bass_utils import run_bass_kernel_spmd

F32 = mybir.dt.float32
BF16 = mybir.dt.bfloat16
AF = mybir.ActivationFunctionType
OP = mybir.AluOpType
AX = mybir.AxisListType

B, S, D = 16, 2048, 1024
NC = 8
SPC = B // NC
H, HD = 16, 64
NE = 8
TPC = SPC * S
TB = 1024
NTB = TPC // TB
KCH = D // 128
QT = S // 128
NEG = -1.0e30

ENGINES = ("tensor", "vector", "scalar", "sync", "gpsimd")
DEBUG = False


class Buf:

    __slots__ = ("ready", "cons")

    def __init__(self):
        self.ready = []
        self.cons = []

    def wr(self):
        w = self.cons + self.ready
        self.cons = []
        return w

    def set_ready(self, sem, val):
        self.ready = [(sem, val)]

    def rd(self):
        return list(self.ready)

    def used(self, sem, val):
        self.cons.append((sem, val))


class Ctx:

    def __init__(self, name, eng, sems, counts, last_wait):
        self.name = name
        self.E = eng
        self.sems = sems
        self.c = counts
        self.lw = last_wait

    def bump(self, sem, by=1):
        self.c[sem] += by
        return self.c[sem]

    def do(self, eng, fn=None, waits=(), inc=None, inc_by=1):
        if self.name == eng:
            for s, v in waits:
                assert v <= self.c[s], (s, v, self.c[s])
                if v > 0 and self.lw.get((eng, s), -1) < v:
                    self.E.wait_ge(self.sems[s], v)
                    self.lw[(eng, s)] = v
            if fn is not None:
                ins = fn(self.E)
                if inc is not None:
                    ins.then_inc(self.sems[inc], inc_by)
        if inc is not None:
            return self.bump(inc, inc_by)
        return None

    def drain(self, eng):
        if self.name == eng:
            self.E.drain()

    DMA_POOLS = {"gpsimd": [f"sG{i}" for i in range(8)],
                 "sync": [f"sN{i}" for i in range(4)]}

    def dma(self, eng, out, in_, waits=(), **kw):
        key = f"_rr_{eng}"
        self.c[key] = self.c.get(key, -1) + 1
        sem = self.DMA_POOLS[eng][self.c[key] % len(self.DMA_POOLS[eng])]
        waits = list(waits) + [(sem, self.c[sem])]
        v = self.do(eng, lambda E: E.dma_start(out=out, in_=in_, **kw),
                    waits=waits, inc=sem, inc_by=16)
        return (sem, v)


def dedup(waits):
    d = {}
    for s, v in waits:
        if v > 0 and v > d.get(s, -1):
            d[s] = v
    return list(d.items())


def build_nc():
    nc = bass.Bass("TRN2", target_bir_lowering=False)

    x_h = nc.dram_tensor("x", [TPC, D], F32, kind="ExternalInput").ap()
    y_h = nc.dram_tensor("y", [TPC, D], F32, kind="ExternalOutput").ap()
    vones_h = nc.dram_tensor("vones", [128, 256], BF16, kind="ExternalInput").ap()
    wq_h = nc.dram_tensor("wq_img", [128, 8192], F32, kind="ExternalInput").ap()
    wk_h = nc.dram_tensor("wk_img", [128, 8192], F32, kind="ExternalInput").ap()
    wv_h = nc.dram_tensor("wv_img", [128, 8192], F32, kind="ExternalInput").ap()
    wo_h = nc.dram_tensor("wo_img", [128, 8192], F32, kind="ExternalInput").ap()
    uw_h = nc.dram_tensor("uw_all", [NE + 1, 128, 32768], F32, kind="ExternalInput").ap()
    dw_h = nc.dram_tensor("dw_all", [NE + 1, 128, 32768], F32, kind="ExternalInput").ap()
    gw_h = nc.dram_tensor("gate_img", [128, 64], F32, kind="ExternalInput").ap()
    ub_h = nc.dram_tensor("ub_img", [128, 288], F32, kind="ExternalInput").ap()
    db_h = nc.dram_tensor("db_row", [1, (NE + 1) * 1024], F32, kind="ExternalInput").ap()
    wob_h = nc.dram_tensor("wob_row", [1, 1024], F32, kind="ExternalInput").ap()
    gb_h = nc.dram_tensor("gb_row", [1, 8], F32, kind="ExternalInput").ap()
    chi_h = nc.dram_tensor("c_hi", [128, S], F32, kind="ExternalInput").ap()
    shi_h = nc.dram_tensor("s_hi", [128, S], F32, kind="ExternalInput").ap()
    idn_h = nc.dram_tensor("idn", [128, 128], BF16, kind="ExternalInput").ap()
    ones_h = nc.dram_tensor("ones_row", [1, 128], BF16, kind="ExternalInput").ap()
    psw_h = nc.dram_tensor("pswap", [128, 128], F32, kind="ExternalInput").ap()
    msk_h = nc.dram_tensor("mask", [128, 128], BF16, kind="ExternalInput").ap()
    cv_h = nc.dram_tensor("cvals", [128, 2], F32, kind="ExternalInput").ap()
    kind_dbg = "ExternalOutput" if DEBUG else "Internal"
    if DEBUG:
        dbg_g = nc.dram_tensor("dbg_g", [NTB, 128, 128], F32, kind="ExternalOutput").ap()
    xnsp_h = nc.dram_tensor("xn_sp", [SPC, 128, KCH, S], BF16, kind=kind_dbg).ap()
    y1sp_h = nc.dram_tensor("y1_sp", [TPC, D], F32, kind=kind_dbg).ap()

    sem_names = (["sT", "sV", "sS"]
                 + [f"sG{i}" for i in range(8)] + [f"sN{i}" for i in range(4)])
    counts = {s: 0 for s in sem_names}
    marks = {}

    import contextlib
    with contextlib.ExitStack() as stack:
        sems = {s: stack.enter_context(nc.semaphore(s)) for s in sem_names}

        def mk(name, shape, dtype):
            return stack.enter_context(nc.sbuf_tensor(name, shape, dtype))

        ones_sb = mk("ones_sb", [1, 128], BF16)
        gw_sb = mk("gw_sb", [128, 64], BF16)
        gb_sb = mk("gb_sb", [1, 8], BF16)
        ub_sb = mk("ub_sb", [128, 288], F32)
        wob_sb = mk("wob_sb", [1, 1024], BF16)
        wsel = mk("wsel", [128, 64], F32)
        sg_sb = mk("sg_sb", [128, 64], F32)
        stats = mk("stats", [128, 64], F32)
        gt1 = mk("gt1", [128, 8], F32)
        gt2 = mk("gt2", [128, 8], F32)
        cv_sb = mk("cv_sb", [128, 2], F32)
        nc.const_aps.register(F32, 1e-5, cv_sb[:, 0:1])
        nc.const_aps.register(F32, 0.125, cv_sb[:, 1:2])

        st_col = {}

        def st(tag):
            if tag not in st_col:
                st_col[tag] = len(st_col)
                assert len(st_col) <= 64, "stats overflow"
            i = st_col[tag]
            return stats[:, i:i + 1]

        def run_block(prog):
            with nc.Block() as blk:
                last_wait = {}
                for en in ENGINES:
                    def make(en_):
                        def f(E):
                            snap = dict(counts)
                            prog(Ctx(en_, E, sems, counts, last_wait))
                            if en_ != ENGINES[-1]:
                                counts.clear()
                                counts.update(snap)
                        return f
                    getattr(blk, en)(make(en))

        with contextlib.ExitStack() as st1:
            def mk1(name, shape, dtype, psum=False):
                f = nc.psum_tensor if psum else nc.sbuf_tensor
                return st1.enter_context(f(name, shape, dtype))

            ctab = mk1("ctab", [128, S], F32)
            stab = mk1("stab", [128, S], F32)
            idn_sb = mk1("idn_sb", [128, 128], BF16)
            psw_sb = mk1("psw_sb", [128, 128], F32)
            msk_sb = mk1("msk_sb", [128, 128], BF16)
            xn_fm = mk1("xn_fm", [128, KCH * S], BF16)
            k_fm = mk1("k_fm", [128, KCH * S], BF16)
            v_tm = mk1("v_tm", [128, QT * H * 65], BF16)
            w_bf = mk1("w_bf", [128, 16384], BF16)
            q_fm = mk1("q_fm", [128, 2 * D], BF16)
            qk32 = mk1("qk32", [128, 1024], F32)
            tt1 = mk1("tt1", [128, 512], F32)
            tt2 = mk1("tt2", [128, 512], F32)
            probs = mk1("probs", [128, 2 * S], BF16)
            attn_tm = mk1("attn_tm", [128, 2 * D], BF16)
            attn_fm = mk1("attn_fm", [128, 2 * D], BF16)
            x_t = mk1("x_t", [128, 2 * D], F32)
            xnt = mk1("xnt", [128, 2 * D], BF16)
            sqd = mk1("sqd", [128, D], F32)
            ps_a = mk1("ps_a", [128, 1024], F32, psum=True)
            ps_b = mk1("ps_b", [128, 1024], F32, psum=True)
            ps_t0 = mk1("ps_t0", [128, 1024], BF16, psum=True)
            ps_t1 = mk1("ps_t1", [128, 1024], BF16, psum=True)
            ps_av0 = mk1("ps_av0", [128, 65], F32, psum=True)
            ps_av1 = mk1("ps_av1", [128, 65], F32, psum=True)
            ps_sc = (ps_a, ps_b)
            ps_t = (ps_t0, ps_t1)
            ps_av = (ps_av0, ps_av1)

            def prog1(c):
                b_psc = [Buf(), Buf()]
                b_pt = [Buf(), Buf()]
                b_pav = [Buf(), Buf()]
                b_xn = Buf()
                b_kfm = Buf()
                b_vtm = Buf()
                b_wslot = [Buf(), Buf()]
                b_qfm = [Buf(), Buf()]
                b_qk32 = Buf()
                b_t1 = Buf()
                b_t2 = Buf()
                b_probs = [Buf(), Buf()]
                b_atm = [Buf(), Buf()]
                b_afm = [Buf(), Buf()]
                b_xt = [Buf(), Buf()]
                b_xnt = [Buf(), Buf()]
                b_stat = {}

                def sb(tag):
                    if tag not in b_stat:
                        b_stat[tag] = Buf()
                    return b_stat[tag]

                ld_c = c.dma("sync", ctab[:], chi_h)
                ld_s = c.dma("sync", stab[:], shi_h)
                ld_i = c.dma("sync", idn_sb[:], idn_h)
                ld_o = c.dma("sync", ones_sb[:], ones_h)
                ld_p = c.dma("sync", psw_sb[:], psw_h)
                ld_m = c.dma("sync", msk_sb[:], msk_h)
                init_x = c.dma("sync", ub_sb[:], ub_h)
                vo_ld = c.dma("sync",
                              v_tm[:].rearrange("p (t h w) -> p t h w", h=H, w=65)[
                                  :, :, :, 64:65],
                              vones_h.rearrange("p (t h w) -> p t h w", h=H, w=1),
                              allow_slow_non_contiguous=True)
                cv_ld = c.dma("sync", cv_sb[:], cv_h)
                c.do("scalar", None, waits=[cv_ld])
                gw_ld = c.dma("gpsimd", gw_sb[:], gw_h)
                gb_ld = c.dma("gpsimd", gb_sb[:], gb_h)
                wob_ld = c.dma("gpsimd", wob_sb[:], wob_h)
                wq_ld = c.dma("gpsimd", w_bf[:, 0:8192], wq_h)
                b_wslot[0].set_ready(*wq_ld)
                marks["ffn_consts"] = [gw_ld, gb_ld, wob_ld]
                tabs_ready = [ld_c, ld_s]
                init_ready = [init_x, ld_i, ld_o, ld_p, ld_m, vo_ld]

                sgc = [0]
                for s in range(SPC):
                    tok0 = s * S

                    for tm in range(QT):
                        sl = tm % 2
                        xt = x_t[:, sl * D:(sl + 1) * D]
                        ld = c.dma("gpsimd", xt,
                                   x_h[tok0 + tm * 128: tok0 + (tm + 1) * 128, :],
                                   waits=dedup(b_xt[sl].wr()))
                        b_xt[sl].set_ready(*ld)
                        sq = c.do("scalar", lambda E, xt=xt, sl=sl: E.activation(
                            sqd[:], xt, AF.Square, accum_out=st(f"ssq{sl}")),
                            waits=dedup(b_xt[sl].rd() + sb(f"ssq{sl}").wr()), inc="sS")
                        sb(f"ssq{sl}").set_ready("sS", sq)
                        b_xt[sl].used("sS", sq)
                        vs = c.do("vector", lambda E, xt=xt, sl=sl: E.reduce_sum(
                            st(f"ssum{sl}"), xt, AX.X),
                            waits=dedup(b_xt[sl].rd() + sb(f"mean{sl}").wr()), inc="sV")
                        b_xt[sl].used("sV", vs)
                        mv_ = c.do("vector", lambda E, sl=sl: E.tensor_scalar_mul(
                            st(f"mean{sl}"), st(f"ssum{sl}"), 1.0 / D),
                            waits=[("sV", vs)], inc="sV")
                        m2_ = c.do("vector", lambda E, sl=sl: E.tensor_mul(
                            st(f"m2{sl}"), st(f"mean{sl}"), st(f"mean{sl}")),
                            waits=[("sV", mv_)], inc="sV")
                        vv = c.do("vector", lambda E, sl=sl: E.tensor_scalar(
                            st(f"var{sl}"), st(f"ssq{sl}"), 1.0 / D,
                            st(f"m2{sl}"), OP.mult, OP.subtract),
                            waits=dedup(sb(f"ssq{sl}").rd() + [("sV", m2_)]), inc="sV")
                        sb(f"ssq{sl}").used("sV", vv)
                        sstd = c.do("scalar", lambda E, sl=sl: E.activation(
                            st(f"std{sl}"), st(f"var{sl}"), AF.Sqrt,
                            bias=1e-5, scale=1.0), waits=[("sV", vv)], inc="sS")
                        rsv = c.do("vector", lambda E, sl=sl: E.reciprocal(
                            st(f"rstd{sl}"), st(f"std{sl}")),
                            waits=[("sS", sstd)], inc="sV")
                        xn_t = xnt[:, sl * D:(sl + 1) * D]
                        sxn = c.do("vector", lambda E, xt=xt, xn_t=xn_t, sl=sl: E.tensor_scalar(
                            xn_t, xt, st(f"mean{sl}"), st(f"rstd{sl}"),
                            OP.subtract, OP.mult),
                            waits=dedup(b_xnt[sl].wr() + b_xt[sl].rd() + [("sV", rsv)]),
                            inc="sV")
                        b_xnt[sl].set_ready("sV", sxn)
                        b_xt[sl].used("sV", sxn)
                        sb(f"mean{sl}").set_ready("sV", sxn)
                        tpos = None
                        w0 = dedup(b_xnt[sl].rd() + b_pt[sl].wr() + init_ready)
                        for j in range(KCH):
                            tpos = c.do("tensor", lambda E, j=j, xn_t=xn_t, sl=sl: E.transpose(
                                ps_t[sl][:, j * 128:(j + 1) * 128],
                                xn_t[:, j * 128:(j + 1) * 128], idn_sb[:]),
                                waits=(w0 if j == 0 else ()),
                                inc=("sT" if j == KCH - 1 else None))
                        b_pt[sl].set_ready("sT", tpos)
                        b_xnt[sl].used("sT", tpos)
                        dst = xn_fm[:].rearrange("p (ch t) -> p ch t", ch=KCH)[
                            :, :, tm * 128:(tm + 1) * 128]
                        srcv = ps_t[sl][:].rearrange("p (ch t) -> p ch t", ch=KCH)
                        ev = c.do("vector", lambda E, dst=dst, srcv=srcv: E.tensor_copy(
                            dst, srcv),
                            waits=dedup(b_pt[sl].rd() + (b_xn.wr() if tm == 0 and s >= 0 else [])),
                            inc="sV")
                        b_pt[sl].used("sV", ev)
                        marks[("xn_tile", s, tm)] = ev
                    xn_done = c.c["sV"]
                    b_xn.set_ready("sV", xn_done)

                    spill = c.dma("sync", xnsp_h[s],
                                  xn_fm[:].rearrange("p (ch t) -> p ch t", ch=KCH),
                                  waits=dedup(b_xn.rd()))
                    b_xn.used(*spill)
                    marks[("xnspill", s)] = spill

                    wk_ld = c.dma("gpsimd", w_bf[:, 8192:16384], wk_h,
                                  waits=dedup(b_wslot[1].wr()))
                    b_wslot[1].set_ready(*wk_ld)
                    kg = 0
                    for ch in range(KCH):
                        for tq in range(4):
                            par = kg % 2
                            pst = ps_sc[par]
                            w0 = dedup(b_wslot[1].rd() + b_xn.rd() + b_psc[par].wr())
                            for k in range(KCH):
                                c.do("tensor", lambda E, ch=ch, tq=tq, k=k, pst=pst: E.matmul(
                                    pst[:, 0:512],
                                    w_bf[:, 8192 + (ch * KCH + k) * 128:
                                         8192 + (ch * KCH + k) * 128 + 128],
                                    xn_fm[:, k * S + tq * 512: k * S + tq * 512 + 512],
                                    start=(k == 0), stop=(k == KCH - 1)),
                                    waits=(w0 if k == 0 else ()),
                                    inc=("sT" if k == KCH - 1 else None))
                            tdone = c.c["sT"]
                            b_psc[par].set_ready("sT", tdone)
                            b_xn.used("sT", tdone)
                            b_wslot[1].used("sT", tdone)
                            ev1 = c.do("vector", lambda E, pst=pst: E.tensor_copy(
                                qk32[:, 0:512], pst[:, 0:512]),
                                waits=dedup(b_psc[par].rd() + b_qk32.wr()), inc="sV")
                            b_qk32.set_ready("sV", ev1)
                            b_psc[par].used("sV", ev1)
                            sw = c.do("tensor", lambda E, pst=pst: E.matmul(
                                pst[:, 512:1024], psw_sb[:], qk32[:, 0:512],
                                start=True, stop=True),
                                waits=dedup(b_qk32.rd() + b_psc[par].wr() + init_ready),
                                inc="sT")
                            b_psc[par].set_ready("sT", sw)
                            b_qk32.used("sT", sw)
                            tok = tq * 512
                            v1 = c.do("vector", lambda E, tok=tok: E.tensor_mul(
                                tt1[:], qk32[:, 0:512], ctab[:, tok:tok + 512]),
                                waits=dedup(tabs_ready + b_t1.wr() + b_qk32.rd()), inc="sV")
                            b_t1.set_ready("sV", v1)
                            b_qk32.used("sV", v1)
                            v2 = c.do("vector", lambda E, pst=pst, tok=tok: E.tensor_mul(
                                tt2[:], pst[:, 512:1024], stab[:, tok:tok + 512]),
                                waits=dedup(b_psc[par].rd() + b_t2.wr()), inc="sV")
                            b_t2.set_ready("sV", v2)
                            b_psc[par].used("sV", v2)
                            ev2 = c.do("vector", lambda E, ch=ch, tok=tok: E.tensor_add(
                                k_fm[:, ch * S + tok: ch * S + tok + 512],
                                tt1[:], tt2[:]),
                                waits=[("sV", v2)], inc="sV")
                            b_t1.used("sV", ev2)
                            b_t2.used("sV", ev2)
                            kg += 1
                    k_ready = c.c["sV"]
                    b_kfm.set_ready("sV", k_ready)

                    wv_ld = c.dma("gpsimd", w_bf[:, 8192:16384], wv_h,
                                  waits=dedup(b_wslot[1].wr()))
                    b_wslot[1].set_ready(*wv_ld)
                    vg = 0
                    for tmv in range(QT):
                        for fh in range(2):
                            par = vg % 2
                            pst = ps_sc[par]
                            w0 = dedup(b_wslot[1].rd() + b_xn.rd() + b_psc[par].wr())
                            for k in range(KCH):
                                c.do("tensor", lambda E, tmv=tmv, fh=fh, k=k, pst=pst: E.matmul(
                                    pst[:, 0:512],
                                    xn_fm[:, k * S + tmv * 128: k * S + tmv * 128 + 128],
                                    w_bf[:, 8192 + k * 1024 + fh * 512:
                                         8192 + k * 1024 + fh * 512 + 512],
                                    start=(k == 0), stop=(k == KCH - 1)),
                                    waits=(w0 if k == 0 else ()),
                                    inc=("sT" if k == KCH - 1 else None))
                            tdone = c.c["sT"]
                            b_psc[par].set_ready("sT", tdone)
                            b_wslot[1].used("sT", tdone)
                            vdst = v_tm[:, tmv * H * 65:(tmv + 1) * H * 65].rearrange(
                                "p (h w) -> p h w", w=65)[:, fh * 8:(fh + 1) * 8, 0:64]
                            ev = c.do("scalar", lambda E, vdst=vdst, pst=pst: E.activation(
                                vdst,
                                pst[:, 0:512].rearrange("p (h w) -> p h w", w=64),
                                AF.Copy),
                                waits=dedup(b_psc[par].rd()), inc="sS")
                            b_psc[par].used("sS", ev)
                            vg += 1
                    v_ready = c.c["sS"]
                    b_vtm.set_ready("sS", v_ready)

                    wo_ld = c.dma("gpsimd", w_bf[:, 8192:16384], wo_h,
                                  waits=dedup(b_wslot[1].wr()))
                    b_wslot[1].set_ready(*wo_ld)

                    for qi in range(QT):
                        qip = qi % 2
                        L = (qi + 1) * 128
                        nch = (L + 1023) // 1024
                        lens = [min(L, 1024), max(0, L - 1024)]

                        w0 = dedup(b_xn.rd() + b_wslot[0].rd() + b_psc[0].wr())
                        for m in range(KCH):
                            for k in range(KCH):
                                c.do("tensor", lambda E, m=m, k=k, qi=qi: E.matmul(
                                    ps_a[:, m * 128:(m + 1) * 128],
                                    w_bf[:, (m * KCH + k) * 128:(m * KCH + k) * 128 + 128],
                                    xn_fm[:, k * S + qi * 128: k * S + qi * 128 + 128],
                                    start=(k == 0), stop=(k == KCH - 1)),
                                    waits=(w0 if (m == 0 and k == 0) else ()),
                                    inc=("sT" if k == KCH - 1 else None))
                        qpdone = c.c["sT"]
                        b_psc[0].set_ready("sT", qpdone)
                        b_wslot[0].used("sT", qpdone)
                        qev = c.do("vector", lambda E: E.tensor_copy(
                            qk32[:, 0:1024], ps_a[:, 0:1024]),
                            waits=dedup(b_psc[0].rd() + b_qk32.wr()), inc="sV")
                        b_qk32.set_ready("sV", qev)
                        b_psc[0].used("sV", qev)
                        w0 = dedup(b_qk32.rd() + b_psc[1].wr())
                        for m in range(KCH):
                            c.do("tensor", lambda E, m=m: E.matmul(
                                ps_b[:, m * 128:(m + 1) * 128], psw_sb[:],
                                qk32[:, m * 128:(m + 1) * 128],
                                start=True, stop=True),
                                waits=(w0 if m == 0 else ()),
                                inc=("sT" if m == KCH - 1 else None))
                        swdone = c.c["sT"]
                        b_psc[1].set_ready("sT", swdone)
                        b_qk32.used("sT", swdone)
                        tok = qi * 128
                        qcomb = None
                        w1 = dedup(b_psc[1].rd() + b_t1.wr() + b_t2.wr()
                                   + b_qfm[qip].wr() + tabs_ready)
                        gprev = None
                        for g in range(2):
                            for m4 in range(4):
                                m = g * 4 + m4
                                ww = w1 if m == 0 else ()
                                if m4 == 0 and gprev is not None:
                                    ww = dedup(list(w1) + [("sV", gprev)])
                                c.do("vector", lambda E, m=m, tok=tok: E.tensor_mul(
                                    tt1[:, m % 4 * 128: m % 4 * 128 + 128],
                                    qk32[:, m * 128:(m + 1) * 128],
                                    ctab[:, tok:tok + 128]),
                                    waits=ww)
                            for m4 in range(4):
                                m = g * 4 + m4
                                mlast = c.do("vector", lambda E, m=m, tok=tok: E.tensor_mul(
                                    tt2[:, m % 4 * 128: m % 4 * 128 + 128],
                                    ps_b[:, m * 128:(m + 1) * 128],
                                    stab[:, tok:tok + 128]),
                                    inc=("sV" if m4 == 3 else None))
                            for m4 in range(4):
                                m = g * 4 + m4
                                qcomb = c.do("vector", lambda E, m=m, qip=qip: E.tensor_add(
                                    q_fm[:, qip * D + m * 128: qip * D + m * 128 + 128],
                                    tt1[:, m % 4 * 128: m % 4 * 128 + 128],
                                    tt2[:, m % 4 * 128: m % 4 * 128 + 128]),
                                    waits=([("sV", mlast)] if m4 == 0 else ()),
                                    inc=("sV" if m4 == 3 else None))
                            gprev = qcomb
                        b_qfm[qip].set_ready("sV", qcomb)
                        b_psc[1].used("sV", qcomb)
                        b_qk32.used("sV", qcomb)
                        b_t1.used("sV", qcomb)
                        b_t2.used("sV", qcomb)

                        ntile = qi + 1
                        ngrp = (ntile + 3) // 4

                        def scoreGrp(h, g):
                            hp = h % 2
                            ch_q = h // 2
                            po = hp * 64
                            j0 = g * 4
                            jn = min(4, ntile - j0)
                            bi = sgc[0] % 2
                            sgc[0] += 1
                            pst = ps_sc[bi]
                            w0 = dedup(b_psc[bi].wr() + b_qfm[qip].rd()
                                       + b_kfm.rd() + init_ready
                                       + (b_probs[hp].wr() if g == 0 else []))
                            for jj in range(jn):
                                j = j0 + jj
                                diag = (j == qi)
                                c.do("tensor", lambda E, jj=jj, j=j, pst=pst, po=po,
                                     ch_q=ch_q, diag=diag: E.matmul(
                                         pst[:, jj * 128: jj * 128 + 128],
                                         k_fm[po:po + 64,
                                              ch_q * S + j * 128: ch_q * S + j * 128 + 128],
                                         q_fm[po:po + 64,
                                              qip * D + ch_q * 128: qip * D + ch_q * 128 + 128],
                                         start=True, stop=not diag,
                                         skip_group_check=True),
                                     waits=(w0 if jj == 0 else ()),
                                     inc=("sT" if (jj == jn - 1 and not diag) else None))
                                if diag:
                                    c.do("tensor", lambda E, jj=jj, pst=pst: E.matmul(
                                        pst[:, jj * 128: jj * 128 + 128],
                                        idn_sb[:], msk_sb[:],
                                        start=False, stop=True,
                                        skip_group_check=True), inc="sT")
                            scd = c.c["sT"]
                            b_psc[bi].set_ready("sT", scd)
                            b_qfm[qip].used("sT", scd)
                            b_kfm.used("sT", scd)
                            ex = c.do("scalar", lambda E, hp=hp, j0=j0, jn=jn, pst=pst:
                                      E.activation(
                                          probs[:, hp * S + j0 * 128:
                                                hp * S + j0 * 128 + jn * 128],
                                          pst[:, 0: jn * 128], AF.Exp, scale=0.125),
                                      waits=dedup(b_psc[bi].rd()), inc="sS")
                            b_psc[bi].used("sS", ex)
                            b_probs[hp].set_ready("sS", ex)
                            return ("sS", ex)

                        def avGrp(h, g, exr):
                            hp = h % 2
                            j0 = g * 4
                            jn = min(4, ntile - j0)
                            w0 = dedup([exr] + b_vtm.rd()
                                       + (b_pav[hp].wr() if g == 0 else []))
                            avd = None
                            for jj in range(jn):
                                j = j0 + jj
                                c.do("tensor", lambda E, j=j, hp=hp, h=h: E.matmul(
                                    ps_av[hp][:, 0:65],
                                    probs[:, hp * S + j * 128: hp * S + j * 128 + 128],
                                    v_tm[:, (j * H + h) * 65: (j * H + h) * 65 + 65],
                                    start=(j == 0), stop=(j == ntile - 1),
                                    skip_group_check=True),
                                    waits=(w0 if jj == 0 else ()),
                                    inc=("sT" if j == ntile - 1 else None))
                            if g == ngrp - 1:
                                avd = c.c["sT"]
                                b_probs[hp].used("sT", avd)
                                b_vtm.used("sT", avd)
                                b_pav[hp].set_ready("sT", avd)

                        def headTail(h):
                            hp = h % 2
                            rcp = c.do("vector", lambda E, hp=hp: E.reciprocal(
                                st(f"rs_{hp}"), ps_av[hp][:, 64:65]),
                                waits=dedup(b_pav[hp].rd() + sb(f"rs_{hp}").wr()),
                                inc="sV")
                            sb(f"rs_{hp}").set_ready("sV", rcp)
                            aev = c.do("vector", lambda E, hp=hp, h=h, qip=qip:
                                       E.tensor_scalar_mul(
                                           attn_tm[:, qip * D + h * 64:
                                                   qip * D + h * 64 + 64],
                                           ps_av[hp][:, 0:64], st(f"rs_{hp}")),
                                       waits=dedup([("sV", rcp)]
                                                   + (b_atm[qip].wr() if h == 0 else [])),
                                       inc="sV")
                            b_pav[hp].used("sV", aev)
                            sb(f"rs_{hp}").used("sV", aev)
                            if h == H - 1:
                                b_atm[qip].set_ready("sV", aev)

                        pend = None
                        for h in range(H):
                            for g in range(ngrp):
                                exr = scoreGrp(h, g)
                                if pend is not None:
                                    avGrp(*pend)
                                    if pend[1] == ngrp - 1:
                                        headTail(pend[0])
                                pend = (h, g, exr)
                        avGrp(*pend)
                        headTail(H - 1)

                        w0 = dedup(b_atm[qip].rd() + b_pt[0].wr())
                        tdone = None
                        for j in range(KCH):
                            tdone = c.do("tensor", lambda E, j=j, qip=qip: E.transpose(
                                ps_t[0][:, j * 128: j * 128 + 128],
                                attn_tm[:, qip * D + j * 128: qip * D + j * 128 + 128],
                                idn_sb[:]),
                                waits=(w0 if j == 0 else ()),
                                inc=("sT" if j == KCH - 1 else None))
                        b_pt[0].set_ready("sT", tdone)
                        b_atm[qip].used("sT", tdone)
                        aev = c.do("vector", lambda E, qip=qip: E.tensor_copy(
                            attn_fm[:, qip * D: qip * D + D], ps_t[0][:, 0:1024]),
                            waits=dedup(b_pt[0].rd() + b_afm[qip].wr()), inc="sV")
                        b_pt[0].used("sV", aev)
                        b_afm[qip].set_ready("sV", aev)

                        xsl = qi % 2
                        xres = x_t[:, xsl * D: xsl * D + D]
                        xl = c.dma("gpsimd", xres,
                                   x_h[tok0 + qi * 128: tok0 + (qi + 1) * 128, :],
                                   waits=dedup(b_xt[xsl].wr()))
                        b_xt[xsl].set_ready(*xl)
                        y1done = None
                        for fh in range(2):
                            pst = ps_sc[fh]
                            w0 = dedup(b_afm[qip].rd() + b_wslot[1].rd() + b_psc[fh].wr())
                            for k in range(KCH):
                                c.do("tensor", lambda E, fh=fh, k=k, qip=qip, pst=pst: E.matmul(
                                    pst[:, 0:512],
                                    attn_fm[:, qip * D + k * 128: qip * D + k * 128 + 128],
                                    w_bf[:, 8192 + k * 1024 + fh * 512:
                                         8192 + k * 1024 + fh * 512 + 512],
                                    start=(k == 0), stop=False),
                                    waits=(w0 if k == 0 else ()))
                            c.do("tensor", lambda E, fh=fh, pst=pst: E.matmul(
                                pst[:, 0:512], ones_sb[:],
                                wob_sb[:, fh * 512: fh * 512 + 512],
                                start=False, stop=True), inc="sT")
                            opdone = c.c["sT"]
                            b_psc[fh].set_ready("sT", opdone)
                            b_wslot[1].used("sT", opdone)
                            y1done = c.do("vector", lambda E, fh=fh, pst=pst,
                                          xres=xres: E.tensor_add(
                                xres[:, fh * 512: fh * 512 + 512],
                                pst[:, 0:512], xres[:, fh * 512: fh * 512 + 512]),
                                waits=dedup(b_psc[fh].rd() + b_xt[xsl].rd()),
                                inc="sV")
                            b_psc[fh].used("sV", y1done)
                        b_xt[xsl].set_ready("sV", y1done)
                        b_afm[qip].used("sT", c.c["sT"])
                        y1st = c.dma("sync",
                                     y1sp_h[tok0 + qi * 128: tok0 + (qi + 1) * 128, :],
                                     xres,
                                     waits=dedup(b_xt[xsl].rd()))
                        b_xt[xsl].used(*y1st)

                marks["attn_end"] = {k: c.c[k] for k in sem_names}

            run_block(prog1)

        with contextlib.ExitStack() as st2:
            def mk2(name, shape, dtype, psum=False):
                f = nc.psum_tensor if psum else nc.sbuf_tensor
                return st2.enter_context(f(name, shape, dtype))

            xn_tb_full = mk2("xn_tb", [128, 2 * KCH * TB], BF16)
            uw_bf = mk2("uw_bf", [128, 2 * 8192], BF16)
            dw_bf0 = mk2("dw_bf0", [128, 16384], BF16)
            dw_bf1 = mk2("dw_bf1", [128, 16384], BF16)
            hact = mk2("hact", [128, 16 * TB], BF16)
            comb = mk2("comb", [128, 8 * D], F32)
            y1f = mk2("y1f", [128, D], F32)
            db_b0 = mk2("db_b0", [1, 1024], BF16)
            db_b1 = mk2("db_b1", [1, 1024], BF16)
            ps_u0 = mk2("ps_u0", [128, 512], F32, psum=True)
            ps_u1 = mk2("ps_u1", [128, 512], F32, psum=True)
            ps_d0 = mk2("ps_d0", [128, 512], F32, psum=True)
            ps_d1 = mk2("ps_d1", [128, 512], F32, psum=True)
            ps_u = (ps_u0, ps_u1)
            ps_d = (ps_d0, ps_d1)
            dw_bufs = (dw_bf0, dw_bf1)

            def prog2(c):
                ae = marks["attn_end"]
                barrier = [(k, ae[k]) for k in sem_names]
                for en in ENGINES:
                    c.do(en, None, waits=barrier)

                b_pu = [Buf(), Buf()]
                b_pd = [Buf(), Buf()]
                b_xntb = [Buf(), Buf()]
                b_uw = [Buf(), Buf()]
                b_dw = [Buf(), Buf()]
                b_hact = Buf()
                b_comb = [Buf() for _ in range(16)]
                b_wsel = Buf()
                b_sg = [Buf(), Buf()]
                b_y1f = [Buf(), Buf()]
                b_g1 = Buf()
                b_g2 = Buf()
                b_stat = {}

                def sb(tag):
                    if tag not in b_stat:
                        b_stat[tag] = Buf()
                    return b_stat[tag]

                b_db = [Buf(), Buf()]
                db_bufs = (db_b0, db_b1)
                ug = 0
                dg = 0
                eg = 0
                eorder = [NE] + list(range(NE))

                for tb in range(NTB):
                    s = tb // 2
                    t0 = (tb % 2) * TB
                    xp = tb % 2
                    xn_tb = xn_tb_full[:, xp * KCH * TB:(xp + 1) * KCH * TB]

                    xnld = c.dma("gpsimd",
                                 xn_tb.rearrange("p (ch t) -> p ch t", ch=KCH),
                                 xnsp_h[s][:, :, t0:t0 + TB],
                                 waits=dedup(b_xntb[xp].wr()
                                             + [marks[("xnspill", s)]]))
                    b_xntb[xp].set_ready(*xnld)

                    for tm in range(8):
                        par = tm % 2
                        pst = ps_u[par]
                        w0 = dedup(b_xntb[xp].rd() + b_pu[par].wr()
                                   + marks["ffn_consts"])
                        for k in range(KCH):
                            c.do("tensor", lambda E, tm=tm, k=k, pst=pst: E.matmul(
                                pst[:, 0:8],
                                xn_tb[:, k * TB + tm * 128: k * TB + tm * 128 + 128],
                                gw_sb[:, k * 8: k * 8 + 8],
                                start=(k == 0), stop=False),
                                waits=(w0 if k == 0 else ()))
                        c.do("tensor", lambda E, pst=pst: E.matmul(
                            pst[:, 0:8], ones_sb[:], gb_sb[:],
                            start=False, stop=True), inc="sT")
                        gdone = c.c["sT"]
                        b_pu[par].set_ready("sT", gdone)
                        b_xntb[xp].used("sT", gdone)
                        sgs = sg_sb[:, tm * 8: tm * 8 + 8]
                        sig = c.do("scalar", lambda E, sgs=sgs, pst=pst: E.activation(
                            sgs, pst[:, 0:8], AF.Sigmoid),
                            waits=dedup(b_pu[par].rd() + b_sg[par].wr()), inc="sS")
                        b_pu[par].used("sS", sig)
                        b_sg[par].set_ready("sS", sig)
                        c.do("vector", lambda E, sgs=sgs, par=par: E.tensor_reduce(
                            st(f"g1_{par}"), sgs, AX.X, OP.max),
                            waits=dedup(b_sg[par].rd() + b_g1.wr() + sb(f"g1_{par}").wr()))
                        c.drain("vector")
                        c.do("vector", lambda E, sgs=sgs, par=par: E.tensor_scalar(
                            gt1[:], sgs, st(f"g1_{par}"), NEG, OP.is_ge, OP.mult))
                        c.drain("vector")
                        c.do("vector", lambda E, sgs=sgs: E.tensor_add(
                            gt2[:], gt1[:], sgs))
                        c.drain("vector")
                        c.do("vector", lambda E, par=par: E.tensor_reduce(
                            st(f"g2_{par}"), gt2[:], AX.X, OP.max))
                        c.drain("vector")
                        c.do("vector", lambda E, sgs=sgs, par=par: E.tensor_single_scalar(
                            gt1[:], sgs, st(f"g2_{par}"), OP.is_ge))
                        c.do("vector", lambda E, par=par: E.tensor_scalar(
                            st(f"gd_{par}"), st(f"g1_{par}"), st(f"g2_{par}"),
                            1e-6, OP.add, OP.add))
                        c.drain("vector")
                        c.do("vector", lambda E, par=par: E.reciprocal(
                            st(f"gr_{par}"), st(f"gd_{par}")))
                        c.do("vector", lambda E, sgs=sgs: E.tensor_mul(
                            gt2[:], sgs, gt1[:]))
                        c.drain("vector")
                        wvm = c.do("vector", lambda E, tm=tm, par=par: E.tensor_scalar(
                            wsel[:, tm * 8: tm * 8 + 8], gt2[:],
                            st(f"gr_{par}"), 0.1, OP.mult, OP.mult), inc="sV")
                        sb(f"g1_{par}").set_ready("sV", wvm)
                        b_sg[par].used("sV", wvm)
                    wsel_done = c.c["sV"]
                    b_wsel.set_ready("sV", wsel_done)
                    if DEBUG:
                        c.dma("sync", dbg_g[tb][:, 0:64], sg_sb[:],
                              waits=[("sV", wsel_done)])
                        c.dma("sync", dbg_g[tb][:, 64:128], wsel[:],
                              waits=[("sV", wsel_done)])

                    for ei, e in enumerate(eorder):
                        dbb = db_bufs[eg % 2]
                        dbld = c.dma("gpsimd", dbb[:],
                                     db_h[0:1, e * 1024: e * 1024 + 1024],
                                     waits=dedup(b_db[eg % 2].wr()))
                        b_db[eg % 2].set_ready(*dbld)
                        for half in range(2):
                            dbuf = dw_bufs[dg % 2]
                            dwld = c.dma("gpsimd", dbuf[:],
                                         dw_h[e][:, half * 16384: half * 16384 + 16384],
                                         waits=dedup(b_dw[dg % 2].wr()))
                            b_dw[dg % 2].set_ready(*dwld)

                            for q in range(2):
                                up = ug % 2
                                uoff = up * 8192
                                uwld = c.dma("gpsimd", uw_bf[:, uoff:uoff + 8192],
                                             uw_h[e][:, (half * 16 + q * 8) * 1024:
                                                     (half * 16 + q * 8) * 1024 + 8192],
                                             waits=dedup(b_uw[up].wr()))
                                b_uw[up].set_ready(*uwld)
                                for ml in range(8):
                                    mg = q * 8 + ml
                                    for th in range(2):
                                        par = (mg * 2 + th) % 2
                                        pst = ps_u[par]
                                        w0 = dedup(b_uw[up].rd() + b_xntb[xp].rd()
                                                   + b_pu[par].wr())
                                        for k in range(KCH):
                                            c.do("tensor", lambda E, k=k, pst=pst, uoff=uoff,
                                                 ml=ml, th=th: E.matmul(
                                                     pst[:],
                                                     uw_bf[:, uoff + (ml * KCH + k) * 128:
                                                           uoff + (ml * KCH + k) * 128 + 128],
                                                     xn_tb[:, k * TB + th * 512:
                                                           k * TB + th * 512 + 512],
                                                     start=(k == 0), stop=(k == KCH - 1)),
                                                 waits=(w0 if k == 0 else ()),
                                                 inc=("sT" if k == KCH - 1 else None))
                                        updone = c.c["sT"]
                                        b_pu[par].set_ready("sT", updone)
                                        b_uw[up].used("sT", updone)
                                        b_xntb[xp].used("sT", updone)
                                        gel = c.do("scalar", lambda E, mg=mg, th=th, e=e,
                                                   half=half, pst=pst: E.activation(
                                            hact[:, mg * TB + th * 512: mg * TB + th * 512 + 512],
                                            pst[:], AF.Gelu,
                                            bias=ub_sb[:, e * 32 + half * 16 + mg:
                                                       e * 32 + half * 16 + mg + 1]),
                                            waits=dedup(b_pu[par].rd() + init_b2), inc="sS")
                                        b_pu[par].used("sS", gel)
                                ug += 1
                            gel_all = c.c["sS"]
                            b_hact.set_ready("sS", gel_all)

                            for tm in range(8):
                                for fh in range(2):
                                    par = (tm * 2 + fh) % 2
                                    pst = ps_d[par]
                                    w0 = dedup(b_dw[dg % 2].rd() + b_hact.rd()
                                               + b_pd[par].wr())
                                    last_kl = 15
                                    for kl in range(16):
                                        is_last = (kl == last_kl) and (half == 0)
                                        c.do("tensor", lambda E, tm=tm, fh=fh, kl=kl,
                                             pst=pst, dbuf=dbuf, is_last=is_last: E.matmul(
                                                 pst[:],
                                                 hact[:, kl * TB + tm * 128:
                                                      kl * TB + tm * 128 + 128],
                                                 dbuf[:, kl * 1024 + fh * 512:
                                                      kl * 1024 + fh * 512 + 512],
                                                 start=(kl == 0), stop=is_last),
                                             waits=(w0 if kl == 0 else ()),
                                             inc=("sT" if (is_last) else None))
                                    if half == 1:
                                        c.do("tensor", lambda E, fh=fh, pst=pst, dbb=dbb: E.matmul(
                                            pst[:], ones_sb[:],
                                            dbb[:, fh * 512: fh * 512 + 512],
                                            start=False, stop=True),
                                            waits=dedup(b_db[eg % 2].rd()), inc="sT")
                                    dndone = c.c["sT"]
                                    b_pd[par].set_ready("sT", dndone)
                                    b_dw[dg % 2].used("sT", dndone)
                                    b_hact.used("sT", dndone)
                                    ci = tm * 2 + fh
                                    cslice = comb[:, tm * D + fh * 512: tm * D + fh * 512 + 512]
                                    if ei == 0 and half == 0:
                                        wts = dedup(b_pd[par].rd() + b_comb[ci].wr())
                                        cm = c.do("vector", lambda E, cslice=cslice, pst=pst:
                                                  E.tensor_scalar_mul(cslice, pst[:], 0.1),
                                                  waits=wts, inc="sV")
                                    elif e == NE:
                                        wts = dedup(b_pd[par].rd() + b_comb[ci].rd())
                                        cm = c.do("vector", lambda E, cslice=cslice, pst=pst:
                                                  E.scalar_tensor_tensor(
                                                      cslice, pst[:], 0.1, cslice,
                                                      OP.mult, OP.add),
                                                  waits=wts, inc="sV")
                                    else:
                                        wcol = wsel[:, tm * 8 + e: tm * 8 + e + 1]
                                        wts = dedup(b_pd[par].rd() + b_comb[ci].rd()
                                                    + b_wsel.rd())
                                        cm = c.do("vector", lambda E, cslice=cslice, pst=pst,
                                                  wcol=wcol: E.scalar_tensor_tensor(
                                                      cslice, pst[:], wcol, cslice,
                                                      OP.mult, OP.add),
                                                  waits=wts, inc="sV")
                                    b_comb[ci].set_ready("sV", cm)
                                    b_pd[par].used("sV", cm)
                                    b_wsel.used("sV", cm)
                            if half == 1:
                                b_db[eg % 2].used("sT", c.c["sT"])
                            dg += 1

                        eg += 1

                    for tm in range(8):
                        qp = tm % 2
                        yl = c.dma("gpsimd", y1f[:],
                                   y1sp_h[tb * TB + tm * 128: tb * TB + (tm + 1) * 128, :],
                                   waits=dedup(b_y1f[0].wr()))
                        b_y1f[0].set_ready(*yl)
                        cslice = comb[:, tm * D: tm * D + D]
                        ci0, ci1 = tm * 2, tm * 2 + 1
                        w0 = dedup(b_comb[ci0].rd() + b_comb[ci1].rd()
                                   + sb(f"ma_{qp}").wr())
                        c.do("vector", lambda E, cslice=cslice, qp=qp: E.tensor_reduce(
                            st(f"ma_{qp}"), cslice, AX.X, OP.max,
                            apply_absolute_value=True), waits=w0)
                        c.drain("vector")
                        c.do("vector", lambda E, qp=qp: E.tensor_scalar_add(
                            st(f"ma_{qp}"), st(f"ma_{qp}"), 1e-6))
                        c.drain("vector")
                        c.do("vector", lambda E, qp=qp: E.reciprocal(
                            st(f"mr_{qp}"), st(f"ma_{qp}")))
                        c.drain("vector")
                        fin = c.do("vector", lambda E, cslice=cslice, qp=qp: E.scalar_tensor_tensor(
                            cslice, cslice, st(f"mr_{qp}"), y1f[:],
                            OP.mult, OP.add),
                            waits=dedup(b_y1f[0].rd()), inc="sV")
                        sb(f"ma_{qp}").set_ready("sV", fin)
                        b_y1f[0].used("sV", fin)
                        b_comb[ci0].set_ready("sV", fin)
                        b_comb[ci1].set_ready("sV", fin)
                        yst = c.dma("sync",
                                    y_h[tb * TB + tm * 128: tb * TB + (tm + 1) * 128, :],
                                    cslice, waits=dedup([("sV", fin)]))
                        b_comb[ci0].used(*yst)
                        b_comb[ci1].used(*yst)

                fw = [(f"sN{i}", c.c[f"sN{i}"]) for i in range(4)]
                c.do("sync", None, waits=fw)
                c.do("tensor", None, waits=fw)

            init_b2 = []
            run_block(prog2)

    return nc



def _images(i):
    f32 = np.float32
    out = {}

    def fm_lhsT_image(w):
        m = w.shape[1] // 128
        return np.ascontiguousarray(
            w.reshape(KCH, 128, m, 128).transpose(1, 2, 0, 3).reshape(128, m * KCH * 128))

    def tm_rhs_image(w):
        n = w.shape[1]
        return np.ascontiguousarray(
            w.reshape(KCH, 128, n).transpose(1, 0, 2).reshape(128, KCH * n))

    out["wq_img"] = fm_lhsT_image(np.asarray(i["L0_attn_wq"], f32))
    out["wk_img"] = fm_lhsT_image(np.asarray(i["L0_attn_wk"], f32))
    out["wv_img"] = tm_rhs_image(np.asarray(i["L0_attn_wv"], f32))
    out["wo_img"] = tm_rhs_image(np.asarray(i["L0_attn_wo_w"], f32))

    uw = np.asarray(i["L0_ffn_up_w"], f32)
    uwi = uw.reshape(NE, KCH, 128, 32, 128).transpose(0, 2, 3, 1, 4).reshape(NE, 128, 32768)
    swu = np.asarray(i["L0_ffn_sw_up"], f32)
    swui = swu.reshape(KCH, 128, 32, 128).transpose(1, 2, 0, 3).reshape(1, 128, 32768)
    out["uw_all"] = np.ascontiguousarray(np.concatenate([uwi, swui], axis=0))

    dw = np.asarray(i["L0_ffn_down_w"], f32)
    dwi = dw.reshape(NE, 2, 16, 128, 1024).transpose(0, 3, 1, 2, 4).reshape(NE, 128, 32768)
    swd = np.asarray(i["L0_ffn_sw_down"], f32)
    swdi = swd.reshape(2, 16, 128, 1024).transpose(2, 0, 1, 3).reshape(1, 128, 32768)
    out["dw_all"] = np.ascontiguousarray(np.concatenate([dwi, swdi], axis=0))

    gw = np.asarray(i["L0_ffn_gate_w"], f32)
    out["gate_img"] = np.ascontiguousarray(
        gw.reshape(KCH, 128, 8).transpose(1, 0, 2).reshape(128, 64))

    ub = np.asarray(i["L0_ffn_up_b"], f32)
    ubi = ub.reshape(NE, 32, 128).transpose(2, 0, 1).reshape(128, NE * 32)
    sbu = np.asarray(i["L0_ffn_sb_up"], f32).reshape(32, 128).T
    out["ub_img"] = np.ascontiguousarray(np.concatenate([ubi, sbu], axis=1))

    db = np.asarray(i["L0_ffn_down_b"], f32).reshape(1, NE * 1024)
    sbd = np.asarray(i["L0_ffn_sb_down"], f32).reshape(1, 1024)
    out["db_row"] = np.ascontiguousarray(np.concatenate([db, sbd], axis=1))
    out["wob_row"] = np.asarray(i["L0_attn_wo_b"], f32).reshape(1, 1024).copy()
    out["gb_row"] = np.asarray(i["L0_ffn_bias"], f32).reshape(1, 8).copy()

    out["cvals"] = np.ascontiguousarray(
        np.tile(np.array([[1e-5, 0.125]], np.float32), (128, 1)))
    out["idn"] = np.eye(128, dtype=ml_dtypes.bfloat16)
    out["ones_row"] = np.ones((1, 128), dtype=ml_dtypes.bfloat16)
    psw = np.zeros((128, 128), f32)
    for k in range(128):
        psw[k, k ^ 1] = 1.0
    out["pswap"] = psw
    msk = np.zeros((128, 128), f32)
    msk[np.tril_indices(128, -1)] = NEG
    out["mask"] = msk.astype(ml_dtypes.bfloat16)
    out["vones"] = np.ones((128, 256), dtype=ml_dtypes.bfloat16)

    import jax
    import jax.numpy as jnp
    with jax.default_device(jax.devices("cpu")[0]):
        inv = 1.0 / 10000.0 ** (jnp.arange(0, HD, 2)[: HD // 2].astype(jnp.float32) / HD)
        ang = jnp.outer(jnp.arange(S, dtype=jnp.float32), inv)
        cosv = np.asarray(jnp.cos(ang))
        sinv = np.asarray(jnp.sin(ang))
    p = np.arange(128)
    d = p % HD
    jidx = d // 2
    C = cosv[:, jidx].T
    S2 = np.where((d % 2 == 0)[:, None], -sinv[:, jidx].T, sinv[:, jidx].T)
    out["c_hi"] = np.ascontiguousarray(C.astype(f32))
    out["s_hi"] = np.ascontiguousarray(S2.astype(f32))
    return out


def kernel(**inputs):
    imgs = _images(inputs)
    x = np.asarray(inputs["L0_attn_x"], np.float32)
    nc = build_nc()
    in_maps = []
    for cix in range(NC):
        m = dict(imgs)
        m["x"] = np.ascontiguousarray(
            x[cix * SPC:(cix + 1) * SPC].reshape(TPC, D))
        in_maps.append(m)
    res = run_bass_kernel_spmd(nc, in_maps, core_ids=list(range(NC)))
    y = np.stack([r["y"] for r in res.results], axis=0)
    return np.ascontiguousarray(y.reshape(B, S, D))
